# version 14
# baseline (speedup 1.0000x reference)
import numpy as np
import ml_dtypes
from contextlib import ExitStack
import concourse.bass as bass
import concourse.mybir as mybir
from concourse.bass_utils import run_bass_kernel_spmd

F32 = mybir.dt.float32
BF16 = mybir.dt.bfloat16
AF = mybir.ActivationFunctionType
ALU = mybir.AluOpType
AX = mybir.AxisListType

S_LEN = 4096
D = 1024
FF = 4096
T = 512
NSUP = S_LEN // T
NH, NKV, HD = 16, 4, 64
NPAIR = 8
EPS = 1e-5
QW = 1024


class Sem:
    def __init__(self, S, nc, name):
        self.h = S(nc.semaphore(name))
        self.n = 0


def build_program(do_b=True, do_c=True):
    nc = bass.Bass("TRN2", target_bir_lowering=False)

    xT = nc.dram_tensor("xT", [D, S_LEN], F32, kind="ExternalInput").ap()
    wqT = nc.dram_tensor("wqT", [D, D], F32, kind="ExternalInput").ap()
    wkT = nc.dram_tensor("wkT", [D, 256], F32, kind="ExternalInput").ap()
    wvT = nc.dram_tensor("wvT", [D, 256], F32, kind="ExternalInput").ap()
    woT = nc.dram_tensor("woT", [D, D], F32, kind="ExternalInput").ap()
    w1T = nc.dram_tensor("w1T", [FF, D], F32, kind="ExternalInput").ap()
    w2T = nc.dram_tensor("w2T", [FF, D], F32, kind="ExternalInput").ap()
    b1in = nc.dram_tensor("b1in", [32, 128], F32, kind="ExternalInput").ap()
    b2in = nc.dram_tensor("b2in", [8, 128], F32, kind="ExternalInput").ap()
    identin = nc.dram_tensor("identin", [128, 128], F32, kind="ExternalInput").ap()
    onesin = nc.dram_tensor("onesin", [1, 128], F32, kind="ExternalInput").ap()
    masksin = nc.dram_tensor("masksin", [128, 4 * T], BF16, kind="ExternalInput").ap()
    selin = nc.dram_tensor("selin", [16, 8 * 128], F32, kind="ExternalInput").ap()
    outT = nc.dram_tensor("outT", [D, S_LEN], F32, kind="ExternalOutput").ap()
    wq1s = nc.dram_tensor("wq1s", [FF, D], BF16, kind="Internal")
    wq2s = nc.dram_tensor("wq2s", [FF, D], BF16, kind="Internal")
    smrow = nc.dram_tensor("smrow", [8, 2 * T], F32, kind="Internal")

    es = ExitStack()
    S = es.enter_context

    kT = S(nc.sbuf_tensor("kT", [128, 2 * S_LEN], BF16))
    vc = S(nc.sbuf_tensor("vc", [128, 32 * 260], BF16))
    wq_sb = S(nc.sbuf_tensor("wq_sb", [128, 8 * D], BF16))
    wo_sb = S(nc.sbuf_tensor("wo_sb", [128, 8 * D], BF16))
    wk_sb = S(nc.sbuf_tensor("wk_sb", [128, 8 * 256], BF16))
    wv_sb = S(nc.sbuf_tensor("wv_sb", [128, 8 * 256], BF16))
    xs = S(nc.sbuf_tensor("xs", [128, 2, 8 * T], F32))
    yT = S(nc.sbuf_tensor("yT", [128, 8 * T], BF16))
    hT = S(nc.sbuf_tensor("hT", [128, 32 * T], BF16))
    qT = S(nc.sbuf_tensor("qT", [128, 8 * T], BF16))
    cT = S(nc.sbuf_tensor("cT", [128, 8 * T], BF16))
    pT = S(nc.sbuf_tensor("pT", [128, 4 * T], BF16))
    w1b = S(nc.sbuf_tensor("w1b", [128, 3, 1024], BF16))
    w2b = S(nc.sbuf_tensor("w2b", [128, 3, 1024], BF16))
    qsl = S(nc.sbuf_tensor("qsl", [128, 2 * QW], F32))
    qob = S(nc.sbuf_tensor("qob", [128, QW], BF16))
    masks = S(nc.sbuf_tensor("masks", [128, 4 * T], BF16))
    tnorm = S(nc.sbuf_tensor("tnorm", [128, T], F32))
    rbb = S(nc.sbuf_tensor("rbb", [128, T], F32))
    ident = S(nc.sbuf_tensor("ident", [128, 128], F32))
    ones_f = S(nc.sbuf_tensor("ones_f", [1, 128], F32))
    ones_b = S(nc.sbuf_tensor("ones_b", [128, 1], BF16))
    ones_cf = S(nc.sbuf_tensor("ones_cf", [128, 1], F32))
    b1sb = S(nc.sbuf_tensor("b1sb", [32, 128], F32))
    b2sb = S(nc.sbuf_tensor("b2sb", [8, 128], F32))
    b1h = S(nc.sbuf_tensor("b1h", [128, 32], F32))
    b2h = S(nc.sbuf_tensor("b2h", [128, 8], F32))
    gacc = S(nc.sbuf_tensor("gacc", [128, 1], F32))
    gtmp = S(nc.sbuf_tensor("gtmp", [128, 1], F32))
    thr2 = S(nc.sbuf_tensor("thr2", [128, 2], F32))
    sums16 = S(nc.sbuf_tensor("sums16", [16, T], F32))
    selm = S(nc.sbuf_tensor("selm", [16, 8 * 128], F32))
    qrow = S(nc.sbuf_tensor("qrow", [1, 144], F32))
    statr = S(nc.sbuf_tensor("statr", [1, 2 * T], F32))
    rr = statr

    pbSS = S(nc.psum_tensor("pbSS", [128, 4 * T], F32))
    pbC = [S(nc.psum_tensor(f"pbC{i}", [128, T], F32)) for i in range(4)]
    pb = [pbSS[:, 0:T], pbSS[:, T:2 * T], pbSS[:, 2 * T:3 * T], pbSS[:, 3 * T:4 * T],
          pbC[0][:], pbC[1][:], pbC[2][:], pbC[3][:]]

    sW = Sem(S, nc, "sW")
    sCN = Sem(S, nc, "sCN")
    sw1 = [Sem(S, nc, "sw1a"), Sem(S, nc, "sw1b"), Sem(S, nc, "sw1c")]
    sw2 = [Sem(S, nc, "sw2a"), Sem(S, nc, "sw2b"), Sem(S, nc, "sw2c")]
    sxl = [Sem(S, nc, "sxla"), Sem(S, nc, "sxlb")]
    sq = [Sem(S, nc, "sq0"), Sem(S, nc, "sq1")]
    sq2 = [Sem(S, nc, "sq2a"), Sem(S, nc, "sq2b")]
    sqg = Sem(S, nc, "sqg")
    sw1g = Sem(S, nc, "sw1g")
    sw2g = Sem(S, nc, "sw2g")
    sW2 = Sem(S, nc, "sW2")
    sSM = Sem(S, nc, "sSM")
    sGP = Sem(S, nc, "sGP")
    sAW = Sem(S, nc, "sAW")
    sOUT = Sem(S, nc, "sOUT")
    sTE = Sem(S, nc, "sTE")
    sSC = Sem(S, nc, "sSC")
    sVE = Sem(S, nc, "sVE")

    EN = ("sy", "gp", "te", "sc", "ve")
    ops = {e: [] for e in EN}

    def emit(e, f):
        ops[e].append(f)

    def W(e, sem, v=None):
        v = sem.n if v is None else v
        assert v <= sem.n, "wait on future count"
        if v > 0:
            emit(e, lambda eng, s=sem.h, v=v: eng.wait_ge(s, v))

    def I(e, f, sem=None, k=1, selfwait=True):
        if selfwait and e == "ve" and sVE.n > 0:
            emit(e, lambda eng, v=sVE.n: eng.wait_ge(sVE.h, v))
        elif selfwait and e == "sc" and sSC.n > 0:
            emit(e, lambda eng, v=sSC.n: eng.wait_ge(sSC.h, v))
        if sem is None:
            emit(e, f)
        else:
            sem.n += k
            emit(e, lambda eng, f=f, s=sem.h, k=k: f(eng).then_inc(s, k))

    for dst, src in ((ident, identin), (ones_f, onesin),
                     (b1sb, b1in), (b2sb, b2in), (masks, masksin), (selm, selin)):
        I("sy", lambda eng, d=dst, s=src: eng.dma_start(out=d[:], in_=s), sCN, 16)
    for dc in range(8):
        I("gp", lambda eng, dc=dc: eng.dma_start(
            out=wq_sb[:, dc * D:(dc + 1) * D], in_=wqT[dc * 128:(dc + 1) * 128, :]), sAW, 16)
        I("gp", lambda eng, dc=dc: eng.dma_start(
            out=wo_sb[:, dc * D:(dc + 1) * D], in_=woT[dc * 128:(dc + 1) * 128, :]), sAW, 16)
        I("gp", lambda eng, dc=dc: eng.dma_start(
            out=wk_sb[:, dc * 256:(dc + 1) * 256], in_=wkT[dc * 128:(dc + 1) * 128, :]), sAW, 16)
        I("gp", lambda eng, dc=dc: eng.dma_start(
            out=wv_sb[:, dc * 256:(dc + 1) * 256], in_=wvT[dc * 128:(dc + 1) * 128, :]), sAW, 16)
    xsrc = xT.rearrange("(dc p) c -> p dc c", p=128)

    def load_x(s, buf):
        I("gp", lambda eng, s=s, buf=buf: eng.dma_start(
            out=xs[:, buf, :], in_=xsrc[:, :, s * T:(s + 1) * T]), sxl[buf], 16)

    load_x(0, 0)
    I("ve", lambda eng: eng.memset(vc[:], 1.0), sVE, 1)
    I("ve", lambda eng: eng.memset(ones_b[:], 1.0), sVE, 1)
    I("ve", lambda eng: eng.memset(ones_cf[:], 1.0), sVE, 1)
    I("ve", lambda eng: eng.memset(gacc[:], 0.0), sVE, 1)

    W("te", sCN)
    I("te", lambda eng: eng.transpose(pb[4][:, 0:32], b1sb[:], ident[0:32, 0:32]), sTE, 1)
    I("te", lambda eng: eng.transpose(pb[5][:, 0:8], b2sb[:], ident[0:8, 0:8]), sTE, 1)
    W("sc", sTE)
    I("sc", lambda eng: eng.activation(out=b1h[:], in_=pb[4][:, 0:32], func=AF.Copy, scale=0.5), sSC, 1)
    I("sc", lambda eng: eng.activation(out=b2h[:], in_=pb[5][:, 0:8], func=AF.Copy, scale=0.5), sSC, 1)

    qstate = {"vcons": [0, 0], "qi": 0}

    def quantize_passA(wsrc, nslab, ldqs, lsems):
        vcons = qstate["vcons"]
        for i in range(nslab):
            par = qstate["qi"] % 2
            qstate["qi"] += 1
            W(ldqs[par], sVE, vcons[par])
            I(ldqs[par], lambda eng, i=i, w=wsrc, par=par: eng.dma_start(
                out=qsl[:, par * QW:(par + 1) * QW],
                in_=w[i * 128:(i + 1) * 128, :]), lsems[par], 16)
            W("ve", lsems[par])
            I("ve", lambda eng, par=par: eng.reduce_sum(
                out=gtmp[:], in_=qsl[:, par * QW:(par + 1) * QW],
                axis=AX.X, apply_absolute_value=True), sVE, 1)
            I("ve", lambda eng: eng.tensor_add(out=gacc[:], in0=gacc[:], in1=gtmp[:]), sVE, 1)
            vcons[par] = sVE.n
        grow = qrow[0:1, 0:128]
        W("te", sVE)
        I("te", lambda eng: eng.transpose(pbC[3][0:1, 0:128], gacc[:], ident[:]), sTE, 1)
        W("sc", sTE)
        I("sc", lambda eng: eng.copy(out=grow, in_=pbC[3][0:1, 0:128]), sSC, 1)
        W("ve", sSC)
        kk = 64
        while kk >= 1:
            I("ve", lambda eng, kk=kk: eng.tensor_add(
                out=qrow[0:1, 0:kk], in0=qrow[0:1, 0:kk], in1=qrow[0:1, kk:2 * kk]), sVE, 1)
            kk //= 2
        I("ve", lambda eng: eng.tensor_scalar(
            out=qrow[0:1, 128:129], in0=qrow[0:1, 0:1], scalar1=1.0 / (FF * 1024.0),
            scalar2=EPS, op0=ALU.mult, op1=ALU.add), sVE, 1)
        I("ve", lambda eng: eng.tensor_scalar(
            out=qrow[0:1, 128:129], in0=qrow[0:1, 128:129], scalar1=0.5,
            scalar2=None, op0=ALU.mult), sVE, 1)
        I("ve", lambda eng: eng.tensor_scalar(
            out=qrow[0:1, 129:130], in0=qrow[0:1, 128:129], scalar1=-1.0,
            scalar2=None, op0=ALU.mult), sVE, 1)
        I("ve", lambda eng: eng.memset(gacc[:], 0.0), sVE, 1)
        W("te", sVE)
        I("te", lambda eng: eng.matmul(pbC[3][:, 0:2], ones_f[:], qrow[0:1, 128:130],
                                       start=True, stop=True), sTE, 1)
        W("sc", sTE)
        I("sc", lambda eng: eng.copy(out=thr2[:], in_=pbC[3][:, 0:2]), sSC, 1)
        W("ve", sSC)

    def quantize_passB(wsrc, nslab, ldqs, lsems, wsem, wq, write_fn, per_slab_cb=None):
        vcons = qstate["vcons"]
        for i in range(nslab):
            par = qstate["qi"] % 2
            qstate["qi"] += 1
            W(ldqs[par], sVE, vcons[par])
            I(ldqs[par], lambda eng, i=i, w=wsrc, par=par: eng.dma_start(
                out=qsl[:, par * QW:(par + 1) * QW],
                in_=w[i * 128:(i + 1) * 128, :]), lsems[par], 16)
            W("ve", lsems[par])
            W("ve", wsem)
            qs = qsl[:, par * QW:(par + 1) * QW]
            I("ve", lambda eng, qs=qs: eng.tensor_scalar(
                out=qob[:], in0=qs, scalar1=thr2[:, 0:1], scalar2=None, op0=ALU.is_gt), sVE, 1)
            I("ve", lambda eng, qs=qs: eng.tensor_scalar(
                out=qs, in0=qs, scalar1=thr2[:, 1:2], scalar2=None, op0=ALU.is_lt), sVE, 1)
            I("ve", lambda eng, qs=qs: eng.tensor_sub(
                out=qob[:], in0=qob[:], in1=qs), sVE, 1)
            vcons[par] = sVE.n
            W(wq, sVE)
            write_fn(i)
            if per_slab_cb is not None:
                per_slab_cb(i)

    def w1_write(i):
        I("gp", lambda eng, i=i: eng.dma_start(
            out=wq1s.ap()[i * 128:(i + 1) * 128, :], in_=qob[:, 0:D]), sW, 16)

    def w2_write(i):
        I("gp", lambda eng, i=i: eng.dma_start(
            out=wq2s.ap()[i * 128:(i + 1) * 128, :], in_=qob[:]), sW2, 16)

    def emit_ln(buf, xupc=None):
        if xupc is None:
            W("sc", sVE)
            W("te", sVE)
        else:
            W("te", sVE, xupc[1])
        sqc = {}
        for dc in range(8):
            xsl = xs[:, buf, dc * T:(dc + 1) * T]
            sq_t = tnorm if dc % 2 == 0 else rbb
            if xupc is not None:
                W("te", sVE, xupc[dc])
            I("te", lambda eng, x=xsl, dc=dc: eng.matmul(
                pb[0][0:1, :], ones_cf[:], x, start=(dc == 0), stop=(dc == 7)), sTE, 1)
            if xupc is not None:
                W("sc", sVE, xupc[dc])
            if dc >= 2:
                W("sc", sTE, sqc[dc - 2])
            I("sc", lambda eng, x=xsl, sq_t=sq_t: eng.activation(
                out=sq_t[:], in_=x, func=AF.Square), sSC, 1)
            W("te", sSC)
            I("te", lambda eng, sq_t=sq_t, dc=dc: eng.matmul(
                pb[1][0:1, :], ones_cf[:], sq_t[:], start=(dc == 0), stop=(dc == 7)), sTE, 1)
            sqc[dc] = sTE.n
        MEAN = statr[0:1, 0:T]
        RSTD = statr[0:1, T:2 * T]
        MR = sums16[0:1, 0:T]
        W("ve", sTE)
        I("ve", lambda eng: eng.tensor_scalar(out=MEAN, in0=pb[0][0:1, :],
                                              scalar1=1.0 / D, scalar2=None, op0=ALU.mult), sVE, 1)
        I("ve", lambda eng: eng.tensor_mul(out=RSTD, in0=MEAN, in1=MEAN), sVE, 1)
        I("ve", lambda eng: eng.tensor_scalar(out=RSTD, in0=RSTD, scalar1=EPS,
                                              scalar2=None, op0=ALU.subtract), sVE, 1)
        I("ve", lambda eng: eng.scalar_tensor_tensor(
            out=RSTD, in0=pb[1][0:1, :], scalar=1.0 / D, in1=RSTD,
            op0=ALU.mult, op1=ALU.subtract), sVE, 1)
        W("sc", sVE)
        I("sc", lambda eng: eng.activation(out=MR, in_=RSTD, func=AF.Ln), sSC, 1)
        I("sc", lambda eng: eng.activation(out=RSTD, in_=MR, func=AF.Exp, scale=-0.5), sSC, 1)
        W("ve", sSC)
        I("ve", lambda eng: eng.tensor_mul(out=MR, in0=MEAN, in1=RSTD), sVE, 1)
        W("te", sVE)
        I("te", lambda eng: eng.matmul(pb[2][:], ones_f[:], RSTD, start=True, stop=True), sTE, 1)
        I("te", lambda eng: eng.matmul(pb[3][:], ones_f[:], MR, start=True, stop=True), sTE, 1)
        ynormc = []
        W("ve", sTE)
        for dc in range(8):
            xsl = xs[:, buf, dc * T:(dc + 1) * T]
            ysl = yT[:, dc * T:(dc + 1) * T]
            I("ve", lambda eng, x=xsl: eng.tensor_mul(out=tnorm[:], in0=x, in1=pb[2][:]), sVE, 1)
            I("ve", lambda eng, y=ysl: eng.tensor_sub(out=y, in0=tnorm[:], in1=pb[3][:]), sVE, 1)
            ynormc.append(sVE.n)
        return ynormc

    def emit_ffn_pre(s, buf, xupc, chase=False):
        ynormc = emit_ln(buf, xupc)
        htend = {}
        reluc = {}

        def h_chunk(ft):
            bufw = ft % 3
            if chase:
                W("sy", sW, 16 * (ft + 1))
            if ft >= 3:
                W("sy", sTE, htend[ft - 3])
            I("sy", lambda eng, ft=ft, bufw=bufw: eng.dma_start(
                out=w1b[:, bufw, :], in_=wq1s.ap()[ft * 128:(ft + 1) * 128, :]), sw1[bufw], 16)
            W("te", sw1[bufw])
            if ft >= 3:
                W("te", sSC, reluc[ft - 3])
            for dc in range(8):
                if ft == 0:
                    W("te", sVE, ynormc[dc])
                I("te", lambda eng, dc=dc, bufw=bufw: eng.matmul(
                    pb[4 + bufw][:], w1b[:, bufw, dc * 128:(dc + 1) * 128],
                    yT[:, dc * T:(dc + 1) * T], start=(dc == 0), stop=(dc == 7)),
                  sTE, 1 if dc == 7 else 0)
            htend[ft] = sTE.n
            W("sc", sTE)
            I("sc", lambda eng, ft=ft, bufw=bufw: eng.activation(
                out=hT[:, ft * T:(ft + 1) * T], in_=pb[4 + bufw][:], func=AF.Relu,
                bias=b1h[:, ft:ft + 1], scale=0.5), sSC, 1)
            reluc[ft] = sSC.n

        if chase:
            return ynormc, h_chunk
        for ft in range(32):
            h_chunk(ft)
        return sSC.n

    def emit_ffn_post(s, buf, HC, chase=False):
        XUP = sVE.n
        outend = {}
        for fc in range(32):
            bufw = fc % 3
            if chase:
                W("sy", sW2, 16 * (fc + 1))
            if fc >= 3:
                W("sy", sTE, outend[fc - 3])
            I("sy", lambda eng, fc=fc, bufw=bufw: eng.dma_start(
                out=w2b[:, bufw, :], in_=wq2s.ap()[fc * 128:(fc + 1) * 128, :]), sw2[bufw], 16)
            W("te", sw2[bufw])
            if fc == 0:
                W("te", sSC, HC)
                W("te", sVE, XUP)
            for m in range(8):
                I("te", lambda eng, fc=fc, m=m, bufw=bufw: eng.matmul(
                    pb[m][:], w2b[:, bufw, m * 128:(m + 1) * 128],
                    hT[:, fc * T:(fc + 1) * T], start=(fc == 0), stop=(fc == 31)),
                  sTE, 1 if m == 7 else 0)
            outend[fc] = sTE.n
        xupc = []
        W("ve", sTE, outend[31])
        for m in range(8):
            xsl = xs[:, buf, m * T:(m + 1) * T]
            I("ve", lambda eng, m=m, x=xsl: eng.scalar_tensor_tensor(
                out=x, in0=pb[m][:], scalar=b2h[:, m:m + 1], in1=x,
                op0=ALU.add, op1=ALU.add), sVE, 1)
            xupc.append(sVE.n)
        return xupc

    def emit_attn(s, buf, xupc):
        ynormc = emit_ln(buf, xupc)
        W("te", sAW)
        W("ve", sCN)
        for t in range(2):
            if t == 1:
                W("te", sVE)
            for dc in range(8):
                if t == 0:
                    W("te", sVE, ynormc[dc])
                I("te", lambda eng, t=t, dc=dc: eng.matmul(
                    pb[4][:], wk_sb[:, dc * 256 + t * 128: dc * 256 + (t + 1) * 128],
                    yT[:, dc * T:(dc + 1) * T], start=(dc == 0), stop=(dc == 7)),
                  sTE, 1 if dc == 7 else 0)
            W("ve", sTE)
            I("ve", lambda eng, t=t, s=s: eng.tensor_copy(
                out=kT[:, t * S_LEN + s * T: t * S_LEN + (s + 1) * T], in_=pb[4][:]), sVE, 1)
        for tt in range(4):
            kt = 4 * s + tt
            if tt > 0:
                W("te", sVE)
            for dc in range(8):
                I("te", lambda eng, tt=tt, dc=dc: eng.matmul(
                    pb[5][:, 0:256], yT[:, dc * T + tt * 128: dc * T + (tt + 1) * 128],
                    wv_sb[:, dc * 256:(dc + 1) * 256], start=(dc == 0), stop=(dc == 7)),
                  sTE, 1 if dc == 7 else 0)
            W("ve", sTE)
            dstv = vc[:, kt * 260: (kt + 1) * 260].rearrange("p (g w) -> p g w", w=65)[:, :, 0:64]
            srcv = pb[5][:, 0:256].rearrange("p (g w) -> p g w", w=64)
            I("ve", lambda eng, d=dstv, sr=srcv: eng.tensor_copy(out=d, in_=sr), sVE, 1)
        for p in range(8):
            if p > 0:
                W("te", sVE)
            for dc in range(8):
                I("te", lambda eng, p=p, dc=dc: eng.matmul(
                    pb[6][:], wq_sb[:, dc * D + p * 128: dc * D + (p + 1) * 128],
                    yT[:, dc * T:(dc + 1) * T], start=(dc == 0), stop=(dc == 7)),
                  sTE, 1 if dc == 7 else 0)
            W("ve", sTE)
            I("ve", lambda eng, p=p: eng.tensor_copy(
                out=qT[:, p * T:(p + 1) * T], in_=pb[6][:]), sVE, 1)
        KVQ = sVE.n
        njt = 4 * s + 4
        prev_norm = {}
        expc = {}
        maskc = {}
        ctxc = {}
        scorec = {}

        def emit_S(p, j):
            base = (j % 2) * 2 * T
            tbase = (0 if p < 4 else 1) * S_LEN
            if p == 0 and j == 0:
                W("te", sVE, KVQ)
            if (p, j) != (0, 0) and (p, j) != (0, 1):
                fi = p * njt + j - 2
                W("te", sSC, expc[(fi // njt, fi % njt)])
            kA = kT[0:64, tbase + j * 128: tbase + (j + 1) * 128]
            kB = kT[64:128, tbase + j * 128: tbase + (j + 1) * 128]
            qA = qT[0:64, p * T:(p + 1) * T]
            qB = qT[64:128, p * T:(p + 1) * T]
            I("te", lambda eng, kA=kA, qA=qA, base=base: eng.matmul(
                pbSS[:, base:base + T], kA, qA, start=True, stop=True,
                tile_position=(0, 0)), sTE, 0)
            I("te", lambda eng, kB=kB, qB=qB, base=base: eng.matmul(
                pbSS[:, base + T:base + 2 * T], kB, qB, start=True, stop=True,
                tile_position=(64, 0)), sTE, 1)
            scorec[(p, j)] = sTE.n

        def emit_exp(p, j):
            base = (j % 2) * 2 * T
            slot = (j % 2) * 2
            W("sc", sTE, scorec[(p, j)])
            I("sc", lambda eng, base=base, slot=slot: eng.activation(
                out=pT[:, slot * T:(slot + 2) * T], in_=pbSS[:, base:base + 2 * T],
                func=AF.Exp, scale=0.125), sSC, 1, selfwait=False)
            expc[(p, j)] = sSC.n
            m = j - 4 * s
            if m >= 0:
                W("ve", sSC, expc[(p, j)])
                I("ve", lambda eng, slot=slot, m=m: eng.tensor_mul(
                    out=pT[:, slot * T + m * 128:(slot + 1) * T],
                    in0=pT[:, slot * T + m * 128:(slot + 1) * T],
                    in1=masks[:, m * T + m * 128:(m + 1) * T]), sVE, 1)
                I("ve", lambda eng, slot=slot, m=m: eng.tensor_mul(
                    out=pT[:, (slot + 1) * T + m * 128:(slot + 2) * T],
                    in0=pT[:, (slot + 1) * T + m * 128:(slot + 2) * T],
                    in1=masks[:, m * T + m * 128:(m + 1) * T]), sVE, 1)
                maskc[(p, j)] = sVE.n

        def emit_ctx(p, j):
            slot = (j % 2) * 2
            cb = 4 if (p % 2 == 0) else 6
            gA, gB = (0, 2) if p < 4 else (1, 3)
            m = j - 4 * s
            if (p, j) in maskc:
                W("te", sVE, maskc[(p, j)])
            else:
                W("te", sSC, expc[(p, j)])
            if j == 0 and p >= 2 and (p - 2) in prev_norm:
                W("te", sVE, prev_norm[p - 2])
            q0 = m * 128 if m >= 1 else 0
            vA = vc[:, j * 260 + gA * 65: j * 260 + gA * 65 + 65]
            vB = vc[:, j * 260 + gB * 65: j * 260 + gB * 65 + 65]
            I("te", lambda eng, vA=vA, slot=slot, cb=cb, j=j, q0=q0: eng.matmul(
                pb[cb][0:65, q0:T], vA, pT[:, slot * T + q0:(slot + 1) * T],
                start=(j == 0), stop=(j == njt - 1)), sTE, 0)
            I("te", lambda eng, vB=vB, slot=slot, cb=cb, j=j, q0=q0: eng.matmul(
                pb[cb + 1][0:65, q0:T], vB, pT[:, (slot + 1) * T + q0:(slot + 2) * T],
                start=(j == 0), stop=(j == njt - 1)), sTE, 1)
            ctxc[(p, j)] = sTE.n

        def emit_tail(p):
            cb = 4 if (p % 2 == 0) else 6
            W("ve", sTE, ctxc[(p, njt - 1)])
            W("ve", sSM)
            I("ve", lambda eng, cb=cb: eng.tensor_copy(out=rr[0:1, 0:T], in_=pb[cb][64:65, :]), sVE, 1)
            I("ve", lambda eng, cb=cb: eng.tensor_copy(out=rr[0:1, T:2 * T], in_=pb[cb + 1][64:65, :]), sVE, 1)
            dstA = cT[0:64, p * T:(p + 1) * T]
            dstB = cT[64:128, p * T:(p + 1) * T]
            I("ve", lambda eng, d=dstA, cb=cb: eng.tensor_copy(out=d, in_=pb[cb][0:64, :]), sVE, 1)
            I("ve", lambda eng, d=dstB, cb=cb: eng.tensor_copy(out=d, in_=pb[cb + 1][0:64, :]), sVE, 1)
            prev_norm[p] = sVE.n
            W("gp", sVE)
            I("gp", lambda eng, p=p: eng.dma_start(out=smrow.ap()[p], in_=rr[0:1, 0:2 * T]), sSM, 16)

        iters = [(p, j) for p in range(NPAIR) for j in range(njt)]
        for idx, (p, j) in enumerate(iters):
            emit_S(p, j)
            emit_exp(p, j)
            if idx >= 1:
                pp, jj = iters[idx - 1]
                emit_ctx(pp, jj)
                if jj == njt - 1:
                    emit_tail(pp)
        emit_ctx(NPAIR - 1, njt - 1)
        emit_tail(NPAIR - 1)
        W("gp", sSM)
        I("gp", lambda eng: eng.dma_start(
            out=sums16[:], in_=smrow.ap().rearrange("r (two c) -> (r two) c", c=T)), sSM, 16)
        W("ve", sSM)
        I("ve", lambda eng: eng.reciprocal(out=sums16[:], in_=sums16[:]), sVE, 1)
        RDONE = sVE.n
        rbevac = {}
        normc = {}
        for p in range(NPAIR):
            rbk = pbC[2] if (p % 2 == 0) else pbC[3]
            if p == 0:
                W("te", sVE, RDONE)
            if p >= 2:
                W("te", sSC, rbevac[p - 2])
            I("te", lambda eng, p=p, rbk=rbk: eng.matmul(
                rbk[:], selm[:, p * 128:(p + 1) * 128], sums16[:],
                start=True, stop=True), sTE, 1)
            W("sc", sTE)
            W("sc", sVE)
            I("sc", lambda eng, rbk=rbk: eng.copy(out=rbb[:], in_=rbk[:]), sSC, 1)
            rbevac[p] = sSC.n
            W("ve", sSC)
            dstA = cT[0:64, p * T:(p + 1) * T]
            dstB = cT[64:128, p * T:(p + 1) * T]
            I("ve", lambda eng, d=dstA: eng.tensor_mul(out=d, in0=d, in1=rbb[0:64, :]), sVE, 1)
            I("ve", lambda eng, d=dstB: eng.tensor_mul(out=d, in0=d, in1=rbb[64:128, :]), sVE, 1)
            normc[p] = sVE.n
        woend = {}
        for ct in range(8):
            W("te", sVE, normc[ct])
            if ct == 0:
                W("te", sSC)
            for m in range(6):
                I("te", lambda eng, m=m, ct=ct: eng.matmul(
                    pb[m][:], wo_sb[:, ct * D + m * 128: ct * D + (m + 1) * 128],
                    cT[:, ct * T:(ct + 1) * T], start=(ct == 0), stop=(ct == 7)),
                  sTE, 1 if m == 5 else 0)
        W("te", sSC, rbevac[7])
        for ct in range(8):
            for m in (6, 7):
                I("te", lambda eng, m=m, ct=ct: eng.matmul(
                    pb[m][:], wo_sb[:, ct * D + m * 128: ct * D + (m + 1) * 128],
                    cT[:, ct * T:(ct + 1) * T], start=(ct == 0), stop=(ct == 7)),
                  sTE, 1 if m == 7 else 0)
        xupc = []
        W("ve", sTE)
        for m in range(8):
            xsl = xs[:, buf, m * T:(m + 1) * T]
            I("ve", lambda eng, m=m, x=xsl: eng.tensor_add(out=x, in0=x, in1=pb[m][:]), sVE, 1)
            xupc.append(sVE.n)
        return xupc

    quantize_passA(w1T, 32, ("sy", "gp"), (sq[0], sqg))
    outdst = outT.rearrange("(dc p) c -> p dc c", p=128)
    for s in range(NSUP):
        buf = s % 2
        W("sc", sxl[buf])
        W("ve", sxl[buf])
        W("te", sxl[buf])
        if s + 1 < NSUP:
            if s >= 1:
                W("gp", sOUT, 16 * s)
            load_x(s + 1, (s + 1) % 2)
        if s == 0:
            ynormc0, h_chunk = emit_ffn_pre(0, 0, None, chase=True)
            quantize_passB(w1T, 32, ("sy", "gp"), (sq[0], sqg), sW, "gp",
                           w1_write, per_slab_cb=h_chunk)
            HC = sSC.n
            quantize_passA(w2T, 32, ("gp", "gp"), sq2)
            quantize_passB(w2T, 32, ("gp", "gp"), sq2, sW2, "gp", w2_write)
            xupc = emit_ffn_post(s, buf, HC, chase=True)
        else:
            HC = emit_ffn_pre(s, buf, None)
            xupc = emit_ffn_post(s, buf, HC)
        if do_b:
            xupc = emit_attn(s, buf, xupc)
        if do_c:
            HC = emit_ffn_pre(s, buf, xupc)
            xupc = emit_ffn_post(s, buf, HC)
        W("gp", sVE)
        I("gp", lambda eng, s=s, buf=buf: eng.dma_start(
            out=outdst[:, :, s * T:(s + 1) * T], in_=xs[:, buf, :]), sOUT, 16)
    W("gp", sOUT)

    with es:
        with nc.Block() as block:
            @block.sync
            def _(sync):
                for f in ops["sy"]:
                    f(sync)

            @block.gpsimd
            def _(gpsimd):
                for f in ops["gp"]:
                    f(gpsimd)

            @block.tensor
            def _(tensor):
                for f in ops["te"]:
                    f(tensor)

            @block.scalar
            def _(scalar):
                for f in ops["sc"]:
                    f(scalar)

            @block.vector
            def _(vector):
                for f in ops["ve"]:
                    f(vector)
    return nc


def host_inputs(L0_attn_x, L0_attn_wq, L0_attn_wk, L0_attn_wv, L0_attn_wo,
                L0_ffn_W1, L0_ffn_b1, L0_ffn_W2, L0_ffn_b2):
    bf = ml_dtypes.bfloat16
    x = np.asarray(L0_attn_x, dtype=np.float32)
    wq = np.asarray(L0_attn_wq, dtype=np.float32)
    wk = np.asarray(L0_attn_wk, dtype=np.float32)
    wv = np.asarray(L0_attn_wv, dtype=np.float32)
    wo = np.asarray(L0_attn_wo, dtype=np.float32)
    W1 = np.asarray(L0_ffn_W1, dtype=np.float32)
    b1 = np.asarray(L0_ffn_b1, dtype=np.float32)
    W2 = np.asarray(L0_ffn_W2, dtype=np.float32)
    b2 = np.asarray(L0_ffn_b2, dtype=np.float32)
    wqT = np.ascontiguousarray(wq.T).reshape(D, NH, HD)
    head_order = [h for p in range(8) for h in (p, p + 8)]
    wqT = np.ascontiguousarray(wqT[:, head_order, :].reshape(D, D))
    wkT = np.ascontiguousarray(wk.T).reshape(D, 4, HD)
    wkT = np.ascontiguousarray(wkT[:, [0, 2, 1, 3], :].reshape(D, 256))
    wvT = np.ascontiguousarray(wv.T)
    woT = np.ascontiguousarray(wo.T).reshape(NH, HD, D)
    woT = np.ascontiguousarray(woT[head_order].reshape(D, D))
    w1T = np.ascontiguousarray(
        W1.reshape(32, 128, 8, 128).transpose(0, 3, 2, 1).reshape(FF, D))
    w2T = np.ascontiguousarray(W2.T)
    b1in = np.ascontiguousarray(b1.reshape(32, 128))
    b2in = np.ascontiguousarray(b2.reshape(8, 128))
    ident = np.eye(128, dtype=np.float32)
    ones = np.ones((1, 128), dtype=np.float32)
    ki = np.arange(128)[:, None]
    qi = np.arange(T)[None, :]
    masksf = np.concatenate([(m * 128 + ki <= qi).astype(np.float32) for m in range(4)], axis=1)
    masksb = masksf.astype(bf)
    selmat = np.zeros((16, 8 * 128), dtype=np.float32)
    for p in range(8):
        selmat[2 * p, p * 128:p * 128 + 64] = 1.0
        selmat[2 * p + 1, p * 128 + 64:p * 128 + 128] = 1.0
    common = dict(wqT=wqT, wkT=wkT, wvT=wvT, woT=woT, w1T=w1T, w2T=w2T,
                  b1in=b1in, b2in=b2in, identin=ident, onesin=ones,
                  masksin=masksb, selin=selmat)
    in_maps = []
    for b in range(8):
        m = dict(common)
        m["xT"] = np.ascontiguousarray(x[b].T)
        in_maps.append(m)
    return in_maps


def kernel(**inputs):
    in_maps = host_inputs(**inputs)
    nc = build_program()
    res = run_bass_kernel_spmd(nc, in_maps, core_ids=list(range(8)))
    out = np.stack([np.asarray(r["outT"]).T for r in res.results], axis=0)
    return np.ascontiguousarray(out)


# version 15
# speedup vs baseline: 1.0419x; 1.0419x over previous
import numpy as np
import ml_dtypes
from contextlib import ExitStack
import concourse.bass as bass
import concourse.mybir as mybir
from concourse.bass_utils import run_bass_kernel_spmd

F32 = mybir.dt.float32
BF16 = mybir.dt.bfloat16
AF = mybir.ActivationFunctionType
ALU = mybir.AluOpType
AX = mybir.AxisListType

S_LEN = 4096
D = 1024
FF = 4096
T = 512
NSUP = S_LEN // T
NH, NKV, HD = 16, 4, 64
NPAIR = 8
EPS = 1e-5
QW = 1024


class Sem:
    def __init__(self, S, nc, name):
        self.h = S(nc.semaphore(name))
        self.n = 0


def build_program(do_b=True, do_c=True):
    nc = bass.Bass("TRN2", target_bir_lowering=False)

    xT = nc.dram_tensor("xT", [D, S_LEN], F32, kind="ExternalInput").ap()
    wqT = nc.dram_tensor("wqT", [D, D], F32, kind="ExternalInput").ap()
    wkT = nc.dram_tensor("wkT", [D, 256], F32, kind="ExternalInput").ap()
    wvT = nc.dram_tensor("wvT", [D, 256], F32, kind="ExternalInput").ap()
    woT = nc.dram_tensor("woT", [D, D], F32, kind="ExternalInput").ap()
    w1T = nc.dram_tensor("w1T", [FF, D], F32, kind="ExternalInput").ap()
    w2T = nc.dram_tensor("w2T", [FF, D], F32, kind="ExternalInput").ap()
    b1in = nc.dram_tensor("b1in", [32, 128], F32, kind="ExternalInput").ap()
    b2in = nc.dram_tensor("b2in", [8, 128], F32, kind="ExternalInput").ap()
    identin = nc.dram_tensor("identin", [128, 128], F32, kind="ExternalInput").ap()
    onesin = nc.dram_tensor("onesin", [1, 128], F32, kind="ExternalInput").ap()
    masksin = nc.dram_tensor("masksin", [128, 4 * T], BF16, kind="ExternalInput").ap()
    selin = nc.dram_tensor("selin", [16, 8 * 128], F32, kind="ExternalInput").ap()
    outT = nc.dram_tensor("outT", [D, S_LEN], F32, kind="ExternalOutput").ap()
    wq1s = nc.dram_tensor("wq1s", [FF, D], BF16, kind="Internal")
    wq2s = nc.dram_tensor("wq2s", [FF, D], BF16, kind="Internal")
    smrow = nc.dram_tensor("smrow", [8, 2 * T], F32, kind="Internal")

    es = ExitStack()
    S = es.enter_context

    kT = S(nc.sbuf_tensor("kT", [128, 2 * S_LEN], BF16))
    vc = S(nc.sbuf_tensor("vc", [128, 32 * 260], BF16))
    wq_sb = S(nc.sbuf_tensor("wq_sb", [128, 8 * D], BF16))
    wo_sb = S(nc.sbuf_tensor("wo_sb", [128, 8 * D], BF16))
    wk_sb = S(nc.sbuf_tensor("wk_sb", [128, 8 * 256], BF16))
    wv_sb = S(nc.sbuf_tensor("wv_sb", [128, 8 * 256], BF16))
    xs = S(nc.sbuf_tensor("xs", [128, 2, 8 * T], F32))
    yT = S(nc.sbuf_tensor("yT", [128, 8 * T], BF16))
    hT = S(nc.sbuf_tensor("hT", [128, 32 * T], BF16))
    qT = S(nc.sbuf_tensor("qT", [128, 8 * T], BF16))
    cT = S(nc.sbuf_tensor("cT", [128, 8 * T], BF16))
    pT = S(nc.sbuf_tensor("pT", [128, 4 * T], BF16))
    w1b = S(nc.sbuf_tensor("w1b", [128, 3, 1024], BF16))
    w2b = S(nc.sbuf_tensor("w2b", [128, 3, 1024], BF16))
    qsl = S(nc.sbuf_tensor("qsl", [128, 2 * QW], F32))
    qob = S(nc.sbuf_tensor("qob", [128, QW], BF16))
    masks = S(nc.sbuf_tensor("masks", [128, 4 * T], BF16))
    tnorm = S(nc.sbuf_tensor("tnorm", [128, T], F32))
    rbb = S(nc.sbuf_tensor("rbb", [128, T], F32))
    ident = S(nc.sbuf_tensor("ident", [128, 128], F32))
    ones_f = S(nc.sbuf_tensor("ones_f", [1, 128], F32))
    ones_b = S(nc.sbuf_tensor("ones_b", [128, 1], BF16))
    ones_cf = S(nc.sbuf_tensor("ones_cf", [128, 1], F32))
    b1sb = S(nc.sbuf_tensor("b1sb", [32, 128], F32))
    b2sb = S(nc.sbuf_tensor("b2sb", [8, 128], F32))
    b1h = S(nc.sbuf_tensor("b1h", [128, 32], F32))
    b2h = S(nc.sbuf_tensor("b2h", [128, 8], F32))
    gacc = S(nc.sbuf_tensor("gacc", [128, 1], F32))
    gtmp = S(nc.sbuf_tensor("gtmp", [128, 1], F32))
    thr2 = S(nc.sbuf_tensor("thr2", [128, 2], F32))
    sums16 = S(nc.sbuf_tensor("sums16", [16, T], F32))
    selm = S(nc.sbuf_tensor("selm", [16, 8 * 128], F32))
    qrow = S(nc.sbuf_tensor("qrow", [1, 144], F32))
    statr = S(nc.sbuf_tensor("statr", [1, 2 * T], F32))
    rr = statr

    pbSS = S(nc.psum_tensor("pbSS", [128, 4 * T], F32))
    pbC = [S(nc.psum_tensor(f"pbC{i}", [128, T], F32)) for i in range(4)]
    pb = [pbSS[:, 0:T], pbSS[:, T:2 * T], pbSS[:, 2 * T:3 * T], pbSS[:, 3 * T:4 * T],
          pbC[0][:], pbC[1][:], pbC[2][:], pbC[3][:]]

    sW = Sem(S, nc, "sW")
    sCN = Sem(S, nc, "sCN")
    sw1 = [Sem(S, nc, "sw1a"), Sem(S, nc, "sw1b"), Sem(S, nc, "sw1c")]
    sw2 = [Sem(S, nc, "sw2a"), Sem(S, nc, "sw2b"), Sem(S, nc, "sw2c")]
    sxl = [Sem(S, nc, "sxla"), Sem(S, nc, "sxlb")]
    sq = [Sem(S, nc, "sq0"), Sem(S, nc, "sq1")]
    sq2 = [Sem(S, nc, "sq2a"), Sem(S, nc, "sq2b")]
    sqg = Sem(S, nc, "sqg")
    sw1g = Sem(S, nc, "sw1g")
    sw2g = Sem(S, nc, "sw2g")
    sW2 = Sem(S, nc, "sW2")
    sSM = Sem(S, nc, "sSM")
    sGP = Sem(S, nc, "sGP")
    sAW = Sem(S, nc, "sAW")
    sOUT = Sem(S, nc, "sOUT")
    sTE = Sem(S, nc, "sTE")
    sSC = Sem(S, nc, "sSC")
    sVE = Sem(S, nc, "sVE")

    EN = ("sy", "gp", "te", "sc", "ve")
    ops = {e: [] for e in EN}

    def emit(e, f):
        ops[e].append(f)

    def W(e, sem, v=None):
        v = sem.n if v is None else v
        assert v <= sem.n, "wait on future count"
        if v > 0:
            emit(e, lambda eng, s=sem.h, v=v: eng.wait_ge(s, v))

    def I(e, f, sem=None, k=1, selfwait=True):
        if selfwait and e == "ve" and sVE.n > 0:
            emit(e, lambda eng, v=sVE.n: eng.wait_ge(sVE.h, v))
        elif selfwait and e == "sc" and sSC.n > 0:
            emit(e, lambda eng, v=sSC.n: eng.wait_ge(sSC.h, v))
        if sem is None:
            emit(e, f)
        else:
            sem.n += k
            emit(e, lambda eng, f=f, s=sem.h, k=k: f(eng).then_inc(s, k))

    for dst, src in ((ident, identin), (ones_f, onesin),
                     (b1sb, b1in), (b2sb, b2in), (masks, masksin), (selm, selin)):
        I("sy", lambda eng, d=dst, s=src: eng.dma_start(out=d[:], in_=s), sCN, 16)
    for dst, srcw, wd in ((wq_sb, wqT, D), (wo_sb, woT, D), (wk_sb, wkT, 256), (wv_sb, wvT, 256)):
        I("gp", lambda eng, d=dst, w=srcw, wd=wd: eng.dma_start(
            out=d[:], in_=w.rearrange("(dc p) c -> p dc c", p=128)), sAW, 16)
    xsrc = xT.rearrange("(dc p) c -> p dc c", p=128)

    def load_x(s, buf):
        I("gp", lambda eng, s=s, buf=buf: eng.dma_start(
            out=xs[:, buf, :], in_=xsrc[:, :, s * T:(s + 1) * T]), sxl[buf], 16)

    load_x(0, 0)
    I("ve", lambda eng: eng.memset(vc[:], 1.0), sVE, 1)
    I("ve", lambda eng: eng.memset(ones_b[:], 1.0), sVE, 1)
    I("ve", lambda eng: eng.memset(ones_cf[:], 1.0), sVE, 1)
    I("ve", lambda eng: eng.memset(gacc[:], 0.0), sVE, 1)

    W("te", sCN)
    I("te", lambda eng: eng.transpose(pb[4][:, 0:32], b1sb[:], ident[0:32, 0:32]), sTE, 1)
    I("te", lambda eng: eng.transpose(pb[5][:, 0:8], b2sb[:], ident[0:8, 0:8]), sTE, 1)
    W("sc", sTE)
    I("sc", lambda eng: eng.activation(out=b1h[:], in_=pb[4][:, 0:32], func=AF.Copy, scale=0.5), sSC, 1)
    I("sc", lambda eng: eng.activation(out=b2h[:], in_=pb[5][:, 0:8], func=AF.Copy, scale=0.5), sSC, 1)

    qstate = {"vcons": [0, 0], "qi": 0}

    def passA_slab(wsrc, i, ldqs, lsems):
        vcons = qstate["vcons"]
        par = qstate["qi"] % 2
        qstate["qi"] += 1
        W(ldqs[par], sVE, vcons[par])
        I(ldqs[par], lambda eng, i=i, w=wsrc, par=par: eng.dma_start(
            out=qsl[:, par * QW:(par + 1) * QW],
            in_=w[i * 128:(i + 1) * 128, :]), lsems[par], 16)
        W("ve", lsems[par])
        I("ve", lambda eng, par=par: eng.reduce_sum(
            out=gtmp[:], in_=qsl[:, par * QW:(par + 1) * QW],
            axis=AX.X, apply_absolute_value=True), sVE, 1)
        I("ve", lambda eng: eng.tensor_add(out=gacc[:], in0=gacc[:], in1=gtmp[:]), sVE, 1)
        vcons[par] = sVE.n

    def quantize_passA(wsrc, nslab, ldqs, lsems):
        for i in range(nslab):
            passA_slab(wsrc, i, ldqs, lsems)
        gamma_finalize()

    def gamma_finalize():
        grow = qrow[0:1, 0:128]
        W("te", sVE)
        I("te", lambda eng: eng.transpose(pbC[3][0:1, 0:128], gacc[:], ident[:]), sTE, 1)
        W("sc", sTE)
        I("sc", lambda eng: eng.copy(out=grow, in_=pbC[3][0:1, 0:128]), sSC, 1)
        W("ve", sSC)
        kk = 64
        while kk >= 1:
            I("ve", lambda eng, kk=kk: eng.tensor_add(
                out=qrow[0:1, 0:kk], in0=qrow[0:1, 0:kk], in1=qrow[0:1, kk:2 * kk]), sVE, 1)
            kk //= 2
        I("ve", lambda eng: eng.tensor_scalar(
            out=qrow[0:1, 128:129], in0=qrow[0:1, 0:1], scalar1=1.0 / (FF * 1024.0),
            scalar2=EPS, op0=ALU.mult, op1=ALU.add), sVE, 1)
        I("ve", lambda eng: eng.tensor_scalar(
            out=qrow[0:1, 128:129], in0=qrow[0:1, 128:129], scalar1=0.5,
            scalar2=None, op0=ALU.mult), sVE, 1)
        I("ve", lambda eng: eng.tensor_scalar(
            out=qrow[0:1, 129:130], in0=qrow[0:1, 128:129], scalar1=-1.0,
            scalar2=None, op0=ALU.mult), sVE, 1)
        I("ve", lambda eng: eng.memset(gacc[:], 0.0), sVE, 1)
        W("te", sVE)
        I("te", lambda eng: eng.matmul(pbC[3][:, 0:2], ones_f[:], qrow[0:1, 128:130],
                                       start=True, stop=True), sTE, 1)
        W("sc", sTE)
        I("sc", lambda eng: eng.copy(out=thr2[:], in_=pbC[3][:, 0:2]), sSC, 1)
        W("ve", sSC)

    def quantize_passB(wsrc, nslab, ldqs, lsems, wsem, wq, write_fn, per_slab_cb=None):
        vcons = qstate["vcons"]
        for i in range(nslab):
            par = qstate["qi"] % 2
            qstate["qi"] += 1
            W(ldqs[par], sVE, vcons[par])
            I(ldqs[par], lambda eng, i=i, w=wsrc, par=par: eng.dma_start(
                out=qsl[:, par * QW:(par + 1) * QW],
                in_=w[i * 128:(i + 1) * 128, :]), lsems[par], 16)
            W("ve", lsems[par])
            W("ve", wsem)
            qs = qsl[:, par * QW:(par + 1) * QW]
            I("ve", lambda eng, qs=qs: eng.tensor_scalar(
                out=qob[:], in0=qs, scalar1=thr2[:, 0:1], scalar2=None, op0=ALU.is_gt), sVE, 1)
            I("ve", lambda eng, qs=qs: eng.tensor_scalar(
                out=qs, in0=qs, scalar1=thr2[:, 1:2], scalar2=None, op0=ALU.is_lt), sVE, 1)
            I("ve", lambda eng, qs=qs: eng.tensor_sub(
                out=qob[:], in0=qob[:], in1=qs), sVE, 1)
            vcons[par] = sVE.n
            W(wq, sVE)
            write_fn(i)
            if per_slab_cb is not None:
                per_slab_cb(i)

    def w1_write(i):
        I("gp", lambda eng, i=i: eng.dma_start(
            out=wq1s.ap()[i * 128:(i + 1) * 128, :], in_=qob[:, 0:D]), sW, 16)

    def w2_write(i):
        I("gp", lambda eng, i=i: eng.dma_start(
            out=wq2s.ap()[i * 128:(i + 1) * 128, :], in_=qob[:]), sW2, 16)

    def emit_ln(buf, xupc=None):
        if xupc is None:
            W("sc", sVE)
            W("te", sVE)
        else:
            W("te", sVE, xupc[1])
        sqc = {}
        for dc in range(8):
            xsl = xs[:, buf, dc * T:(dc + 1) * T]
            sq_t = tnorm if dc % 2 == 0 else rbb
            if xupc is not None:
                W("te", sVE, xupc[dc])
            I("te", lambda eng, x=xsl, dc=dc: eng.matmul(
                pb[0][0:1, :], ones_cf[:], x, start=(dc == 0), stop=(dc == 7)), sTE, 1)
            if xupc is not None:
                W("sc", sVE, xupc[dc])
            if dc >= 2:
                W("sc", sTE, sqc[dc - 2])
            I("sc", lambda eng, x=xsl, sq_t=sq_t: eng.activation(
                out=sq_t[:], in_=x, func=AF.Square), sSC, 1)
            W("te", sSC)
            I("te", lambda eng, sq_t=sq_t, dc=dc: eng.matmul(
                pb[1][0:1, :], ones_cf[:], sq_t[:], start=(dc == 0), stop=(dc == 7)), sTE, 1)
            sqc[dc] = sTE.n
        MEAN = statr[0:1, 0:T]
        RSTD = statr[0:1, T:2 * T]
        MR = sums16[0:1, 0:T]
        W("ve", sTE)
        I("ve", lambda eng: eng.tensor_scalar(out=MEAN, in0=pb[0][0:1, :],
                                              scalar1=1.0 / D, scalar2=None, op0=ALU.mult), sVE, 1)
        I("ve", lambda eng: eng.tensor_mul(out=RSTD, in0=MEAN, in1=MEAN), sVE, 1)
        I("ve", lambda eng: eng.tensor_scalar(out=RSTD, in0=RSTD, scalar1=EPS,
                                              scalar2=None, op0=ALU.subtract), sVE, 1)
        I("ve", lambda eng: eng.scalar_tensor_tensor(
            out=RSTD, in0=pb[1][0:1, :], scalar=1.0 / D, in1=RSTD,
            op0=ALU.mult, op1=ALU.subtract), sVE, 1)
        W("sc", sVE)
        I("sc", lambda eng: eng.activation(out=MR, in_=RSTD, func=AF.Ln), sSC, 1)
        I("sc", lambda eng: eng.activation(out=RSTD, in_=MR, func=AF.Exp, scale=-0.5), sSC, 1)
        W("ve", sSC)
        I("ve", lambda eng: eng.tensor_mul(out=MR, in0=MEAN, in1=RSTD), sVE, 1)
        W("te", sVE)
        I("te", lambda eng: eng.matmul(pb[2][:], ones_f[:], RSTD, start=True, stop=True), sTE, 1)
        I("te", lambda eng: eng.matmul(pb[3][:], ones_f[:], MR, start=True, stop=True), sTE, 1)
        ynormc = []
        W("ve", sTE)
        for dc in range(8):
            xsl = xs[:, buf, dc * T:(dc + 1) * T]
            ysl = yT[:, dc * T:(dc + 1) * T]
            I("ve", lambda eng, x=xsl: eng.tensor_mul(out=tnorm[:], in0=x, in1=pb[2][:]), sVE, 1)
            I("ve", lambda eng, y=ysl: eng.tensor_sub(out=y, in0=tnorm[:], in1=pb[3][:]), sVE, 1)
            ynormc.append(sVE.n)
        return ynormc

    def emit_ffn_pre(s, buf, xupc, chase=False):
        ynormc = emit_ln(buf, xupc)
        htend = {}
        reluc = {}

        def h_chunk(ft):
            bufw = ft % 3
            if chase:
                W("sy", sW, 16 * (ft + 1))
            if ft >= 3:
                W("sy", sTE, htend[ft - 3])
            I("sy", lambda eng, ft=ft, bufw=bufw: eng.dma_start(
                out=w1b[:, bufw, :], in_=wq1s.ap()[ft * 128:(ft + 1) * 128, :]), sw1[bufw], 16)
            W("te", sw1[bufw])
            if ft >= 3:
                W("te", sSC, reluc[ft - 3])
            for dc in range(8):
                if ft == 0:
                    W("te", sVE, ynormc[dc])
                I("te", lambda eng, dc=dc, bufw=bufw: eng.matmul(
                    pb[4 + bufw][:], w1b[:, bufw, dc * 128:(dc + 1) * 128],
                    yT[:, dc * T:(dc + 1) * T], start=(dc == 0), stop=(dc == 7)),
                  sTE, 1 if dc == 7 else 0)
            htend[ft] = sTE.n
            W("sc", sTE)
            I("sc", lambda eng, ft=ft, bufw=bufw: eng.activation(
                out=hT[:, ft * T:(ft + 1) * T], in_=pb[4 + bufw][:], func=AF.Relu,
                bias=b1h[:, ft:ft + 1], scale=0.5), sSC, 1)
            reluc[ft] = sSC.n

        if chase:
            return ynormc, h_chunk
        for ft in range(32):
            h_chunk(ft)
        return sSC.n

    def emit_ffn_post(s, buf, HC, chase=False):
        XUP = sVE.n
        outend = {}

        def out_chunk(fc):
            bufw = fc % 3
            if chase:
                W("sy", sW2, 16 * (fc + 1))
            if fc >= 3:
                W("sy", sTE, outend[fc - 3])
            I("sy", lambda eng, fc=fc, bufw=bufw: eng.dma_start(
                out=w2b[:, bufw, :], in_=wq2s.ap()[fc * 128:(fc + 1) * 128, :]), sw2[bufw], 16)
            W("te", sw2[bufw])
            if fc == 0:
                W("te", sSC, HC)
                W("te", sVE, XUP)
            for m in range(8):
                I("te", lambda eng, fc=fc, m=m, bufw=bufw: eng.matmul(
                    pb[m][:], w2b[:, bufw, m * 128:(m + 1) * 128],
                    hT[:, fc * T:(fc + 1) * T], start=(fc == 0), stop=(fc == 31)),
                  sTE, 1 if m == 7 else 0)
            outend[fc] = sTE.n

        def finish():
            xupc = []
            W("ve", sTE, outend[31])
            for m in range(8):
                xsl = xs[:, buf, m * T:(m + 1) * T]
                I("ve", lambda eng, m=m, x=xsl: eng.scalar_tensor_tensor(
                    out=x, in0=pb[m][:], scalar=b2h[:, m:m + 1], in1=x,
                    op0=ALU.add, op1=ALU.add), sVE, 1)
                xupc.append(sVE.n)
            return xupc

        if chase:
            return out_chunk, finish
        for fc in range(32):
            out_chunk(fc)
        return finish()

    def emit_attn(s, buf, xupc):
        ynormc = emit_ln(buf, xupc)
        W("te", sAW)
        W("ve", sCN)
        for t in range(2):
            kb_ = pb[4 + t]
            for dc in range(8):
                if t == 0:
                    W("te", sVE, ynormc[dc])
                I("te", lambda eng, t=t, dc=dc, kb_=kb_: eng.matmul(
                    kb_[:], wk_sb[:, dc * 256 + t * 128: dc * 256 + (t + 1) * 128],
                    yT[:, dc * T:(dc + 1) * T], start=(dc == 0), stop=(dc == 7)),
                  sTE, 1 if dc == 7 else 0)
            W("ve", sTE)
            I("ve", lambda eng, t=t, s=s, kb_=kb_: eng.tensor_copy(
                out=kT[:, t * S_LEN + s * T: t * S_LEN + (s + 1) * T], in_=kb_[:]), sVE, 1)
        vevac = {}
        for tt in range(4):
            kt = 4 * s + tt
            vb_ = pb[4 + (tt % 2)]
            if tt < 2:
                W("te", sVE)
            else:
                W("te", sVE, vevac[tt - 2])
            for dc in range(8):
                I("te", lambda eng, tt=tt, dc=dc, vb_=vb_: eng.matmul(
                    vb_[:, 0:256], yT[:, dc * T + tt * 128: dc * T + (tt + 1) * 128],
                    wv_sb[:, dc * 256:(dc + 1) * 256], start=(dc == 0), stop=(dc == 7)),
                  sTE, 1 if dc == 7 else 0)
            W("ve", sTE)
            dstv = vc[:, kt * 260: (kt + 1) * 260].rearrange("p (g w) -> p g w", w=65)[:, :, 0:64]
            srcv = vb_[:, 0:256].rearrange("p (g w) -> p g w", w=64)
            I("ve", lambda eng, d=dstv, sr=srcv: eng.tensor_copy(out=d, in_=sr), sVE, 1)
            vevac[tt] = sVE.n
        qevac = {}
        for p in range(8):
            qb_ = pb[5 + (p % 2)]
            if p < 2:
                W("te", sVE)
            else:
                W("te", sVE, qevac[p - 2])
            for dc in range(8):
                I("te", lambda eng, p=p, dc=dc, qb_=qb_: eng.matmul(
                    qb_[:], wq_sb[:, dc * D + p * 128: dc * D + (p + 1) * 128],
                    yT[:, dc * T:(dc + 1) * T], start=(dc == 0), stop=(dc == 7)),
                  sTE, 1 if dc == 7 else 0)
            W("ve", sTE)
            I("ve", lambda eng, p=p, qb_=qb_: eng.tensor_copy(
                out=qT[:, p * T:(p + 1) * T], in_=qb_[:]), sVE, 1)
            qevac[p] = sVE.n
        KVQ = sVE.n
        njt = 4 * s + 4
        prev_norm = {}
        expc = {}
        maskc = {}
        ctxc = {}
        scorec = {}

        def emit_S(p, j):
            base = (j % 2) * 2 * T
            tbase = (0 if p < 4 else 1) * S_LEN
            if p == 0 and j == 0:
                W("te", sVE, KVQ)
            if (p, j) != (0, 0) and (p, j) != (0, 1):
                fi = p * njt + j - 2
                W("te", sSC, expc[(fi // njt, fi % njt)])
            kA = kT[0:64, tbase + j * 128: tbase + (j + 1) * 128]
            kB = kT[64:128, tbase + j * 128: tbase + (j + 1) * 128]
            qA = qT[0:64, p * T:(p + 1) * T]
            qB = qT[64:128, p * T:(p + 1) * T]
            I("te", lambda eng, kA=kA, qA=qA, base=base: eng.matmul(
                pbSS[:, base:base + T], kA, qA, start=True, stop=True,
                tile_position=(0, 0)), sTE, 0)
            I("te", lambda eng, kB=kB, qB=qB, base=base: eng.matmul(
                pbSS[:, base + T:base + 2 * T], kB, qB, start=True, stop=True,
                tile_position=(64, 0)), sTE, 1)
            scorec[(p, j)] = sTE.n

        def emit_exp(p, j):
            base = (j % 2) * 2 * T
            slot = (j % 2) * 2
            W("sc", sTE, scorec[(p, j)])
            I("sc", lambda eng, base=base, slot=slot: eng.activation(
                out=pT[:, slot * T:(slot + 2) * T], in_=pbSS[:, base:base + 2 * T],
                func=AF.Exp, scale=0.125), sSC, 1, selfwait=False)
            expc[(p, j)] = sSC.n
            m = j - 4 * s
            if m >= 0:
                W("ve", sSC, expc[(p, j)])
                I("ve", lambda eng, slot=slot, m=m: eng.tensor_mul(
                    out=pT[:, slot * T + m * 128:(slot + 1) * T],
                    in0=pT[:, slot * T + m * 128:(slot + 1) * T],
                    in1=masks[:, m * T + m * 128:(m + 1) * T]), sVE, 1)
                I("ve", lambda eng, slot=slot, m=m: eng.tensor_mul(
                    out=pT[:, (slot + 1) * T + m * 128:(slot + 2) * T],
                    in0=pT[:, (slot + 1) * T + m * 128:(slot + 2) * T],
                    in1=masks[:, m * T + m * 128:(m + 1) * T]), sVE, 1)
                maskc[(p, j)] = sVE.n

        def emit_ctx(p, j):
            slot = (j % 2) * 2
            cb = 4 if (p % 2 == 0) else 6
            gA, gB = (0, 2) if p < 4 else (1, 3)
            m = j - 4 * s
            if (p, j) in maskc:
                W("te", sVE, maskc[(p, j)])
            else:
                W("te", sSC, expc[(p, j)])
            if j == 0 and p >= 2 and (p - 2) in prev_norm:
                W("te", sVE, prev_norm[p - 2])
            q0 = m * 128 if m >= 1 else 0
            vA = vc[:, j * 260 + gA * 65: j * 260 + gA * 65 + 65]
            vB = vc[:, j * 260 + gB * 65: j * 260 + gB * 65 + 65]
            I("te", lambda eng, vA=vA, slot=slot, cb=cb, j=j, q0=q0: eng.matmul(
                pb[cb][0:65, q0:T], vA, pT[:, slot * T + q0:(slot + 1) * T],
                start=(j == 0), stop=(j == njt - 1)), sTE, 0)
            I("te", lambda eng, vB=vB, slot=slot, cb=cb, j=j, q0=q0: eng.matmul(
                pb[cb + 1][0:65, q0:T], vB, pT[:, (slot + 1) * T + q0:(slot + 2) * T],
                start=(j == 0), stop=(j == njt - 1)), sTE, 1)
            ctxc[(p, j)] = sTE.n

        def emit_tail(p):
            cb = 4 if (p % 2 == 0) else 6
            W("ve", sTE, ctxc[(p, njt - 1)])
            W("ve", sSM)
            I("ve", lambda eng, cb=cb: eng.tensor_copy(out=rr[0:1, 0:T], in_=pb[cb][64:65, :]), sVE, 1)
            I("ve", lambda eng, cb=cb: eng.tensor_copy(out=rr[0:1, T:2 * T], in_=pb[cb + 1][64:65, :]), sVE, 1)
            dstA = cT[0:64, p * T:(p + 1) * T]
            dstB = cT[64:128, p * T:(p + 1) * T]
            I("ve", lambda eng, d=dstA, cb=cb: eng.tensor_copy(out=d, in_=pb[cb][0:64, :]), sVE, 1)
            I("ve", lambda eng, d=dstB, cb=cb: eng.tensor_copy(out=d, in_=pb[cb + 1][0:64, :]), sVE, 1)
            prev_norm[p] = sVE.n
            W("gp", sVE)
            I("gp", lambda eng, p=p: eng.dma_start(out=smrow.ap()[p], in_=rr[0:1, 0:2 * T]), sSM, 16)

        iters = [(p, j) for p in range(NPAIR) for j in range(njt)]
        for idx, (p, j) in enumerate(iters):
            emit_S(p, j)
            emit_exp(p, j)
            if idx >= 1:
                pp, jj = iters[idx - 1]
                emit_ctx(pp, jj)
                if jj == njt - 1:
                    emit_tail(pp)
        emit_ctx(NPAIR - 1, njt - 1)
        emit_tail(NPAIR - 1)
        W("gp", sSM)
        I("gp", lambda eng: eng.dma_start(
            out=sums16[:], in_=smrow.ap().rearrange("r (two c) -> (r two) c", c=T)), sSM, 16)
        W("ve", sSM)
        I("ve", lambda eng: eng.reciprocal(out=sums16[:], in_=sums16[:]), sVE, 1)
        RDONE = sVE.n
        rbevac = {}
        normc = {}
        for p in range(NPAIR):
            rbk = pbC[2] if (p % 2 == 0) else pbC[3]
            if p == 0:
                W("te", sVE, RDONE)
            if p >= 2:
                W("te", sSC, rbevac[p - 2])
            I("te", lambda eng, p=p, rbk=rbk: eng.matmul(
                rbk[:], selm[:, p * 128:(p + 1) * 128], sums16[:],
                start=True, stop=True), sTE, 1)
            W("sc", sTE)
            W("sc", sVE)
            I("sc", lambda eng, rbk=rbk: eng.copy(out=rbb[:], in_=rbk[:]), sSC, 1)
            rbevac[p] = sSC.n
            W("ve", sSC)
            dstA = cT[0:64, p * T:(p + 1) * T]
            dstB = cT[64:128, p * T:(p + 1) * T]
            I("ve", lambda eng, d=dstA: eng.tensor_mul(out=d, in0=d, in1=rbb[0:64, :]), sVE, 1)
            I("ve", lambda eng, d=dstB: eng.tensor_mul(out=d, in0=d, in1=rbb[64:128, :]), sVE, 1)
            normc[p] = sVE.n
        woend = {}
        for ct in range(8):
            W("te", sVE, normc[ct])
            if ct == 0:
                W("te", sSC)
            for m in range(6):
                I("te", lambda eng, m=m, ct=ct: eng.matmul(
                    pb[m][:], wo_sb[:, ct * D + m * 128: ct * D + (m + 1) * 128],
                    cT[:, ct * T:(ct + 1) * T], start=(ct == 0), stop=(ct == 7)),
                  sTE, 1 if m == 5 else 0)
        W("te", sSC, rbevac[7])
        for ct in range(8):
            for m in (6, 7):
                I("te", lambda eng, m=m, ct=ct: eng.matmul(
                    pb[m][:], wo_sb[:, ct * D + m * 128: ct * D + (m + 1) * 128],
                    cT[:, ct * T:(ct + 1) * T], start=(ct == 0), stop=(ct == 7)),
                  sTE, 1 if m == 7 else 0)
        xupc = []
        W("ve", sTE)
        for m in range(8):
            xsl = xs[:, buf, m * T:(m + 1) * T]
            I("ve", lambda eng, m=m, x=xsl: eng.tensor_add(out=x, in0=x, in1=pb[m][:]), sVE, 1)
            xupc.append(sVE.n)
        return xupc

    quantize_passA(w1T, 32, ("sy", "gp"), (sq[0], sqg))
    outdst = outT.rearrange("(dc p) c -> p dc c", p=128)
    for s in range(NSUP):
        buf = s % 2
        W("sc", sxl[buf])
        W("ve", sxl[buf])
        W("te", sxl[buf])
        if s + 1 < NSUP:
            if s >= 1:
                W("gp", sOUT, 16 * s)
            load_x(s + 1, (s + 1) % 2)
        if s == 0:
            ynormc0, h_chunk = emit_ffn_pre(0, 0, None, chase=True)

            def w1b_and_w2a(i):
                h_chunk(i)
                passA_slab(w2T, i, ("gp", "gp"), sq2)
            quantize_passB(w1T, 32, ("sy", "gp"), (sq[0], sqg), sW, "gp",
                           w1_write, per_slab_cb=w1b_and_w2a)
            HC = sSC.n
            gamma_finalize()
            out_chunk, out_finish = emit_ffn_post(s, buf, HC, chase=True)
            quantize_passB(w2T, 32, ("gp", "gp"), sq2, sW2, "gp", w2_write,
                           per_slab_cb=out_chunk)
            xupc = out_finish()
        else:
            HC = emit_ffn_pre(s, buf, None)
            xupc = emit_ffn_post(s, buf, HC)
        if do_b:
            xupc = emit_attn(s, buf, xupc)
        if do_c:
            HC = emit_ffn_pre(s, buf, xupc)
            xupc = emit_ffn_post(s, buf, HC)
        W("gp", sVE)
        I("gp", lambda eng, s=s, buf=buf: eng.dma_start(
            out=outdst[:, :, s * T:(s + 1) * T], in_=xs[:, buf, :]), sOUT, 16)
    W("gp", sOUT)

    with es:
        with nc.Block() as block:
            @block.sync
            def _(sync):
                for f in ops["sy"]:
                    f(sync)

            @block.gpsimd
            def _(gpsimd):
                for f in ops["gp"]:
                    f(gpsimd)

            @block.tensor
            def _(tensor):
                for f in ops["te"]:
                    f(tensor)

            @block.scalar
            def _(scalar):
                for f in ops["sc"]:
                    f(scalar)

            @block.vector
            def _(vector):
                for f in ops["ve"]:
                    f(vector)
    return nc


def host_inputs(L0_attn_x, L0_attn_wq, L0_attn_wk, L0_attn_wv, L0_attn_wo,
                L0_ffn_W1, L0_ffn_b1, L0_ffn_W2, L0_ffn_b2):
    bf = ml_dtypes.bfloat16
    x = np.asarray(L0_attn_x, dtype=np.float32)
    wq = np.asarray(L0_attn_wq, dtype=np.float32)
    wk = np.asarray(L0_attn_wk, dtype=np.float32)
    wv = np.asarray(L0_attn_wv, dtype=np.float32)
    wo = np.asarray(L0_attn_wo, dtype=np.float32)
    W1 = np.asarray(L0_ffn_W1, dtype=np.float32)
    b1 = np.asarray(L0_ffn_b1, dtype=np.float32)
    W2 = np.asarray(L0_ffn_W2, dtype=np.float32)
    b2 = np.asarray(L0_ffn_b2, dtype=np.float32)
    wqT = np.ascontiguousarray(wq.T).reshape(D, NH, HD)
    head_order = [h for p in range(8) for h in (p, p + 8)]
    wqT = np.ascontiguousarray(wqT[:, head_order, :].reshape(D, D))
    wkT = np.ascontiguousarray(wk.T).reshape(D, 4, HD)
    wkT = np.ascontiguousarray(wkT[:, [0, 2, 1, 3], :].reshape(D, 256))
    wvT = np.ascontiguousarray(wv.T)
    woT = np.ascontiguousarray(wo.T).reshape(NH, HD, D)
    woT = np.ascontiguousarray(woT[head_order].reshape(D, D))
    w1T = np.ascontiguousarray(
        W1.reshape(32, 128, 8, 128).transpose(0, 3, 2, 1).reshape(FF, D))
    w2T = np.ascontiguousarray(W2.T)
    b1in = np.ascontiguousarray(b1.reshape(32, 128))
    b2in = np.ascontiguousarray(b2.reshape(8, 128))
    ident = np.eye(128, dtype=np.float32)
    ones = np.ones((1, 128), dtype=np.float32)
    ki = np.arange(128)[:, None]
    qi = np.arange(T)[None, :]
    masksf = np.concatenate([(m * 128 + ki <= qi).astype(np.float32) for m in range(4)], axis=1)
    masksb = masksf.astype(bf)
    selmat = np.zeros((16, 8 * 128), dtype=np.float32)
    for p in range(8):
        selmat[2 * p, p * 128:p * 128 + 64] = 1.0
        selmat[2 * p + 1, p * 128 + 64:p * 128 + 128] = 1.0
    common = dict(wqT=wqT, wkT=wkT, wvT=wvT, woT=woT, w1T=w1T, w2T=w2T,
                  b1in=b1in, b2in=b2in, identin=ident, onesin=ones,
                  masksin=masksb, selin=selmat)
    in_maps = []
    for b in range(8):
        m = dict(common)
        m["xT"] = np.ascontiguousarray(x[b].T)
        in_maps.append(m)
    return in_maps


def kernel(**inputs):
    in_maps = host_inputs(**inputs)
    nc = build_program()
    res = run_bass_kernel_spmd(nc, in_maps, core_ids=list(range(8)))
    out = np.stack([np.asarray(r["outT"]).T for r in res.results], axis=0)
    return np.ascontiguousarray(out)
